# version 01 (final)
# baseline (speedup 1.0000x reference)
import numpy as np
from contextlib import ExitStack

import concourse.bass as bass
import concourse.mybir as mybir
from concourse.bass_utils import run_bass_kernel_spmd

F32 = mybir.dt.float32
BF16 = mybir.dt.bfloat16
AF = mybir.ActivationFunctionType
ALU = mybir.AluOpType

D = 2048
NCD = 16
SEQ = 2048
T_TILE = 1024
N_TILES = SEQ // T_TILE
NSUB = T_TILE // 512
HID = 1408
NHB = 11
NE = 8
DST = 16
NH = 16
NHQ = 8
DK = 128
CH = 128
LN_EPS = 1e-5
L2_EPS = 1e-6
RMS_EPS = 1e-6

ENGINES = ("tensor", "vector", "scalar", "gpsimd", "sync")
SEM_NAMES = ["pe", "dve", "act", "pool", "dxin", "dwa", "dw13", "dw2", "dwq",
             "dcb", "dout", "dcon"]


class Prog:

    def __init__(self):
        self.entries = {e: [] for e in ENGINES}
        self.cnt = {}
        self.sems = {}

    def c(self, sem):
        return self.cnt.get(sem, 0)

    def do(self, eng, fn, waits=(), inc=None, amt=1):
        waits = tuple((s, v) for (s, v) in waits if v > 0)
        for s, v in waits:
            assert v <= self.c(s), f"wait {s}>={v} but only {self.c(s)} planned"
        val_after = None
        if inc is not None:
            self.cnt[inc] = self.c(inc) + amt
            val_after = self.cnt[inc]
        self.entries[eng].append((fn, waits, inc, amt, val_after))
        return val_after

    def wait(self, eng, waits):
        self.do(eng, None, waits=waits)

    def finalize_and_replay(self, block):
        waited = {}
        for eng in ENGINES:
            for (_fn, waits, _inc, _amt, _va) in self.entries[eng]:
                for s, v in waits:
                    waited.setdefault(s, set()).add(v)
        remap = {}
        for s, vals in waited.items():
            sv = sorted(vals)
            remap[s] = {}
            for i, v in enumerate(sv):
                remap[s][v] = i + 1
        is_dma = {s: (s.startswith("d")) for s in self.sems}

        def compile_engine(E, eng):
            for (fn, waits, inc, amt, val_after) in self.entries[eng]:
                for s, v in waits:
                    if is_dma[s]:
                        E.wait_ge(self.sems[s], v)
                    else:
                        E.wait_ge(self.sems[s], remap[s][v])
                if fn is None:
                    continue
                ins = fn(E)
                if inc is not None:
                    if is_dma[inc]:
                        ins.then_inc(self.sems[inc], amt)
                    elif val_after in waited.get(inc, ()):
                        ins.then_inc(self.sems[inc], 1)

        for eng in ENGINES:
            def make(eng=eng):
                def body(E):
                    compile_engine(E, eng)
                return body
            getattr(block, eng)(make())


DEBUG_DUMP = False
SKIP_B_RESID = False
B_STAGE = 99
C_STAGE = 99


def build_nc(n_sub=4):
    nc = bass.Bass("TRN2", target_bir_lowering=False)
    if DEBUG_DUMP:
        dbg_h = nc.dram_tensor("dbg_h", [128, NCD * DST], F32, kind="ExternalOutput").ap()
        dbg_ar = nc.dram_tensor("dbg_ar", [128, 76 * 1024], mybir.dt.uint8,
                                kind="ExternalOutput").ap()

    x_in = nc.dram_tensor("x_fm", [D, SEQ], F32, kind="ExternalInput").ap()
    y_out = nc.dram_tensor("y_fm", [D, SEQ], F32, kind="ExternalOutput").ap()
    wxT = nc.dram_tensor("wxT", [D, 96], F32, kind="ExternalInput").ap()
    wdtT = nc.dram_tensor("wdtT", [64, D], F32, kind="ExternalInput").ap()
    bdt_c = nc.dram_tensor("bdt_c", [128, 16], F32, kind="ExternalInput").ap()
    alog = nc.dram_tensor("alog", [D, DST], F32, kind="ExternalInput").ap()
    dvec_c = nc.dram_tensor("dvec_c", [128, 16], F32, kind="ExternalInput").ap()
    wqkvT = nc.dram_tensor("wqkvT", [D, 4096], F32, kind="ExternalInput").ap()
    wgateT = nc.dram_tensor("wgateT", [D, D], F32, kind="ExternalInput").ap()
    wbetaT = nc.dram_tensor("wbetaT", [D, 16], F32, kind="ExternalInput").ap()
    walphaT = nc.dram_tensor("walphaT", [D, 16], F32, kind="ExternalInput").ap()
    logA_c = nc.dram_tensor("logA_c", [16, 1], F32, kind="ExternalInput").ap()
    dtb_c = nc.dram_tensor("dtb_c", [16, 1], F32, kind="ExternalInput").ap()
    convw_c = nc.dram_tensor("convw_c", [128, 32, 4], F32, kind="ExternalInput").ap()
    rmsw_r = nc.dram_tensor("rmsw_r", [1, 128], F32, kind="ExternalInput").ap()
    woutT = nc.dram_tensor("woutT", [D, D], F32, kind="ExternalInput").ap()
    gateT = nc.dram_tensor("gateT", [D, NE], F32, kind="ExternalInput").ap()
    w1 = nc.dram_tensor("w1", [NE, D, HID], F32, kind="ExternalInput").ap()
    w2 = nc.dram_tensor("w2", [NE, HID, D], F32, kind="ExternalInput").ap()
    w3 = nc.dram_tensor("w3", [NE, D, HID], F32, kind="ExternalInput").ap()
    sw1 = nc.dram_tensor("sw1", [D, HID], F32, kind="ExternalInput").ap()
    sw2 = nc.dram_tensor("sw2", [HID, D], F32, kind="ExternalInput").ap()
    sw3 = nc.dram_tensor("sw3", [D, HID], F32, kind="ExternalInput").ap()

    p = Prog()

    with ExitStack() as ctx:
        def sb(name, shape, dt):
            return ctx.enter_context(nc.sbuf_tensor(name, shape, dt))

        def psum(name, shape, dt):
            return ctx.enter_context(nc.psum_tensor(name, shape, dt))

        X = sb("X", [128, NCD, T_TILE], F32)
        XLN = sb("XLN", [128, NCD, T_TILE], BF16)
        onesb1 = sb("onesb1", [1, 128], BF16)
        onesrow_f = sb("onesrow_f", [1, 128], F32)
        onescol_b = sb("onescol_b", [128, 1], BF16)
        onescol_f = sb("onescol_f", [128, 1], F32)
        ident_b = sb("ident_b", [128, 128], BF16)
        ident_f = sb("ident_f", [128, 128], F32)
        trilS_neg = sb("trilS_neg", [128, 128], F32)
        triuI = sb("triuI", [128, 128], F32)
        sel32b = sb("sel32b", [32, 32, 128], BF16)
        ones128f = sb("ones128f", [128, 128], F32)
        trilBIG = sb("trilBIG", [128, 128], F32)
        triuBIG = sb("triuBIG", [128, 128], F32)
        selrow_last_f = sb("selrow_last_f", [128, 128], F32)
        sel8 = sb("sel8", [8, 8, 128], BF16)
        selrow_last = sb("selrow_last", [128, 128], BF16)
        bdt_s = sb("bdt_s", [128, 16], F32)
        dvec_s = sb("dvec_s", [128, 16], F32)
        acols = sb("acols", [128, NCD, DST], F32)
        gateTb = sb("gateTb", [128, NCD, NE], BF16)
        wbetab = sb("wbetab", [128, NCD, 16], BF16)
        walphab = sb("walphab", [128, NCD, 16], BF16)
        negA_col = sb("negA_col", [16, 1], F32)
        logA_s = sb("logA_s", [16, 1], F32)
        dtb_s = sb("dtb_s", [16, 1], F32)
        convw_s = sb("convw_s", [128, 32, 4], F32)
        rmsw_s = sb("rmsw_s", [1, 128], F32)
        rms_bc = sb("rms_bc", [128, 128], F32)
        fc = sb("fc", [128, 8], F32)
        FC_LNEPS, FC_MHALF, FC_RMSEPS, FC_LNQ, FC_INV128 = 0, 1, 2, 3, 4
        h_ssm = sb("h_ssm", [128, NCD, DST], F32)
        S_st = sb("S_st", [128, NH, 128], F32)
        Sb_st = sb("Sb_st", [128, NH, 128], BF16)
        conv_tail = sb("conv_tail", [128, 32, 3], BF16)

        ARENA_BYTES = 76 * 1024
        arena = sb("arena", [128, ARENA_BYTES], mybir.dt.uint8)

        def av(off_bytes, shape, dt):
            nbytes = int(np.prod(shape)) * mybir.dt.size(dt)
            assert off_bytes + nbytes <= ARENA_BYTES, "arena overflow"
            flat = arena[:, off_bytes:off_bytes + nbytes].bitcast(dt)
            if len(shape) == 1:
                return flat
            pat = " ".join(f"a{i}" for i in range(len(shape)))
            kw = {f"a{i}": s for i, s in enumerate(shape)}
            return flat.rearrange(f"p ({pat}) -> p {pat}", **kw)

        LN_ROWS_OFF = 64 * 1024
        ln_m = av(LN_ROWS_OFF, (T_TILE,), F32)
        ln_r = av(LN_ROWS_OFF + 4096, (T_TILE,), F32)
        ln_t = av(LN_ROWS_OFF + 8192, (T_TILE,), F32)

        PS6 = psum("PS6", [128, 6, 512], F32)
        P1 = psum("P1", [128, 512], F32)
        PSB = psum("PSB", [128, 8, 128], BF16)

        for s in SEM_NAMES:
            p.sems[s] = ctx.enter_context(nc.semaphore("s_" + s))

        block = ctx.enter_context(nc.Block())

        def g(fn, inc=None, amt=1, waits=()):
            p.do("gpsimd", fn, waits=waits, inc=inc, amt=amt)

        g(lambda E: E.memset(arena[:].bitcast(F32), 0.0))
        g(lambda E: E.memset(fc[:, 0:1], LN_EPS))
        g(lambda E: E.memset(fc[:, 1:2], -0.5))
        g(lambda E: E.memset(fc[:, 2:3], RMS_EPS))
        g(lambda E: E.memset(fc[:, 3:4], float(np.log(128.0 ** -0.5))))
        g(lambda E: E.memset(fc[:, 4:5], 1.0 / 128.0))
        g(lambda E: E.memset(onesb1[:], 1.0))
        g(lambda E: E.memset(onesrow_f[:], 1.0))
        g(lambda E: E.memset(onescol_b[:], 1.0))
        g(lambda E: E.memset(onescol_f[:], 1.0))
        g(lambda E: E.memset(ident_b[:], 0.0))
        g(lambda E: E.affine_select(out=ident_b[:], in_=ident_b[:],
                                    compare_op=ALU.not_equal, fill=1.0, base=0,
                                    pattern=[[-1, 128]], channel_multiplier=1))
        g(lambda E: E.memset(ident_f[:], 0.0))
        g(lambda E: E.affine_select(out=ident_f[:], in_=ident_f[:],
                                    compare_op=ALU.not_equal, fill=1.0, base=0,
                                    pattern=[[-1, 128]], channel_multiplier=1))
        g(lambda E: E.memset(trilS_neg[:], -1.0))
        g(lambda E: E.affine_select(out=trilS_neg[:], in_=trilS_neg[:],
                                    compare_op=ALU.is_gt, fill=0.0, base=0,
                                    pattern=[[-1, 128]], channel_multiplier=1))
        g(lambda E: E.memset(triuI[:], 0.0))
        g(lambda E: E.affine_select(out=triuI[:], in_=triuI[:],
                                    compare_op=ALU.is_gt, fill=1.0, base=0,
                                    pattern=[[-1, 128]], channel_multiplier=1))
        for (selt, n) in ((sel32b, 32), (sel8, 8)):
            g(lambda E, selt=selt: E.memset(selt[:], 0.0))
            g(lambda E, selt=selt, n=n: E.affine_select(
                out=selt[:], in_=selt[:], compare_op=ALU.not_equal, fill=1.0,
                base=0, pattern=[[-1, n], [0, 128]], channel_multiplier=1))
        g(lambda E: E.memset(ones128f[:], 1.0))
        g(lambda E: E.memset(trilBIG[:], -1e9))
        g(lambda E: E.affine_select(out=trilBIG[:], in_=trilBIG[:],
                                    compare_op=ALU.is_gt, fill=0.0, base=0,
                                    pattern=[[-1, 128]], channel_multiplier=1))
        g(lambda E: E.memset(triuBIG[:], 0.0))
        g(lambda E: E.affine_select(out=triuBIG[:], in_=triuBIG[:],
                                    compare_op=ALU.is_ge, fill=1e9, base=0,
                                    pattern=[[-1, 128]], channel_multiplier=1))
        g(lambda E: E.memset(selrow_last_f[:], 0.0))
        g(lambda E: E.affine_select(out=selrow_last_f[:], in_=selrow_last_f[:],
                                    compare_op=ALU.not_equal, fill=1.0, base=-127,
                                    pattern=[[0, 128]], channel_multiplier=1))
        g(lambda E: E.memset(h_ssm[:], 0.0))
        g(lambda E: E.memset(S_st[:], 0.0))
        g(lambda E: E.memset(Sb_st[:], 0.0))
        g(lambda E: E.memset(conv_tail[:], 0.0))
        g(lambda E: E.dma_start(out=bdt_s[:], in_=bdt_c), inc="dcon", amt=16)
        g(lambda E: E.dma_start(out=dvec_s[:], in_=dvec_c), inc="dcon", amt=16)
        g(lambda E: E.dma_start(out=acols[:], in_=alog.rearrange("(c p) s -> p c s", p=128)),
          inc="dcon", amt=16)
        g(lambda E: E.dma_start(out=gateTb[:], in_=gateT.rearrange("(c p) e -> p c e", p=128)),
          inc="dcon", amt=16)
        g(lambda E: E.dma_start(out=wbetab[:], in_=wbetaT.rearrange("(c p) h -> p c h", p=128)),
          inc="dcon", amt=16)
        g(lambda E: E.dma_start(out=walphab[:], in_=walphaT.rearrange("(c p) h -> p c h", p=128)),
          inc="dcon", amt=16)
        g(lambda E: E.dma_start(out=logA_s[:], in_=logA_c), inc="dcon", amt=16)
        g(lambda E: E.dma_start(out=dtb_s[:], in_=dtb_c), inc="dcon", amt=16)
        g(lambda E: E.dma_start(out=convw_s[:], in_=convw_c), inc="dcon", amt=16)
        g(lambda E: E.dma_start(out=rmsw_s[:], in_=rmsw_r), inc="dcon", amt=16)
        ncon = p.c("dcon")
        p.do("scalar", lambda E: E.activation(
            out=acols[:].rearrange("p c s -> p (c s)"),
            in_=acols[:].rearrange("p c s -> p (c s)"), func=AF.Exp),
            waits=[("dcon", ncon)], inc="act")
        p.do("vector", lambda E: E.tensor_scalar_mul(
            acols[:].rearrange("p c s -> p (c s)"),
            acols[:].rearrange("p c s -> p (c s)"), -1.0),
            waits=[("act", p.c("act"))], inc="dve")
        p.do("scalar", lambda E: E.activation(out=negA_col[:], in_=logA_s[:], func=AF.Exp),
             waits=[("dcon", ncon)], inc="act")
        p.do("vector", lambda E: E.tensor_scalar_mul(negA_col[:], negA_col[:], -1.0),
             waits=[("act", p.c("act"))], inc="dve")
        p.do("tensor", lambda E: E.matmul(P1[:, 0:128], onesrow_f[:], rmsw_s[:],
                                          start=True, stop=True),
             waits=[("dcon", ncon)], inc="pe")
        p.do("vector", lambda E: E.tensor_copy(out=rms_bc[:], in_=P1[:, 0:128]),
             waits=[("pe", p.c("pe"))], inc="dve")
        PRO_CON = ncon

        def ln_phase():
            xsq = av(0, (2, 512), BF16)
            xt = av(2 * 1024, (512,), F32)
            cnts = {s: p.c(s) for s in ("pe", "dve", "act", "pool")}
            eng_sems = {"tensor": "pe", "vector": "dve", "scalar": "act",
                        "gpsimd": "pool"}
            for eng, own in eng_sems.items():
                p.wait(eng, [(s, v) for s, v in cnts.items() if s != own])
            for sub in range(NSUB):
                nfree = p.c("dve")
                for c in range(NCD):
                    p.do("tensor", lambda E, c=c, sub=sub: E.matmul(
                        P1[0:1, 0:512], onescol_f[:], X[:, c, sub * 512:(sub + 1) * 512],
                        start=(c == 0), stop=(c == NCD - 1)),
                        waits=[("dve", nfree), ("dcon", PRO_CON),
                               ("act", p.c("act"))] if c == 0 else (),
                        inc="pe" if c == NCD - 1 else None)
                npe_mean = p.c("pe")
                sq_pe = [None, None]
                for c in range(NCD):
                    w = []
                    if sq_pe[c % 2] is not None:
                        w.append(("pe", sq_pe[c % 2]))
                    p.do("scalar", lambda E, c=c, sub=sub: E.activation(
                        out=xsq[:, c % 2, :], in_=X[:, c, sub * 512:(sub + 1) * 512],
                        func=AF.Square), waits=w, inc="act")
                    nact = p.c("act")
                    sq_pe[c % 2] = p.do("tensor", lambda E, c=c: E.matmul(
                        PS6[0:1, 5, :], onescol_b[:], xsq[:, c % 2, :],
                        start=(c == 0), stop=(c == NCD - 1)),
                        waits=[("act", nact)] + ([("dve", nfree)] if c == 0 else []),
                        inc="pe")
                npe_sq = p.c("pe")
                m_row = ln_m[0:1, sub * 512:(sub + 1) * 512]
                r_row = ln_r[0:1, sub * 512:(sub + 1) * 512]
                t_row = ln_t[0:1, sub * 512:(sub + 1) * 512]
                p.do("vector", lambda E, m_row=m_row: E.tensor_scalar_mul(
                    m_row, P1[0:1, 0:512], 1.0 / D), waits=[("pe", npe_mean)], inc="dve")
                p.do("vector", lambda E, m_row=m_row, t_row=t_row: E.tensor_mul(
                    t_row, m_row, m_row), inc="dve")
                p.do("vector", lambda E, t_row=t_row: E.scalar_tensor_tensor(
                    out=t_row, in0=PS6[0:1, 5, :], scalar=1.0 / D, in1=t_row,
                    op0=ALU.mult, op1=ALU.subtract),
                    waits=[("pe", npe_sq)], inc="dve")
                ndve_var = p.c("dve")
                p.do("scalar", lambda E, t_row=t_row: E.activation(
                    out=t_row, in_=t_row, func=AF.Ln, bias=fc[0:1, 0:1]),
                    waits=[("dve", ndve_var)], inc="act")
                p.do("scalar", lambda E, t_row=t_row, r_row=r_row: E.activation(
                    out=r_row, in_=t_row, func=AF.Exp, scale=fc[0:1, 1:2]), inc="act")
                nact_r = p.c("act")
                p.do("tensor", lambda E, m_row=m_row: E.matmul(
                    PS6[:, 4, :], onesrow_f[:], m_row, start=True, stop=True),
                    waits=[("dve", ndve_var)], inc="pe")
                p.do("tensor", lambda E, r_row=r_row: E.matmul(
                    P1[:, 0:512], onesrow_f[:], r_row, start=True, stop=True),
                    waits=[("act", nact_r), ("dve", ndve_var)], inc="pe")
                npe_bc = p.c("pe")
                for c in range(NCD):
                    p.do("vector", lambda E, c=c, sub=sub: E.tensor_sub(
                        xt[:], X[:, c, sub * 512:(sub + 1) * 512], PS6[:, 4, :]),
                        waits=[("pe", npe_bc)], inc="dve")
                    p.do("vector", lambda E, c=c, sub=sub: E.tensor_mul(
                        XLN[:, c, sub * 512:(sub + 1) * 512], xt[:], P1[:, 0:512]),
                        inc="dve")
            return p.c("dve")

        def attn_a_phase(xln_done):
            a_xzbc = av(0, (T_TILE,), BF16)
            a_xzb = av(2 * 1024, (T_TILE,), BF16)
            a_dl = av(4 * 1024, (2, T_TILE), F32)
            a_dx = av(12 * 1024, (2, T_TILE), F32)
            a_ab = av(20 * 1024, (T_TILE,), F32)
            a_bx = av(24 * 1024, (T_TILE,), F32)
            a_hs = av(28 * 1024, (T_TILE,), F32)
            a_tm = av(32 * 1024, (2, T_TILE), F32)
            a_y = av(40 * 1024, (2, T_TILE), F32)
            a_sp = av(48 * 1024, (2, T_TILE), F32)
            wxb = av(56 * 1024, (NCD, 96), BF16)
            wdtb_a = av(59 * 1024, (2048,), BF16)

            p.do("gpsimd", lambda E: E.dma_start(
                out=wxb[:], in_=wxT.rearrange("(c p) k -> p c k", p=128)),
                waits=[("dve", xln_done)], inc="dwa", amt=16)
            p.do("gpsimd", lambda E: E.dma_start(out=wdtb_a[0:64, :], in_=wdtT),
                 inc="dwa", amt=16)
            nwa = p.c("dwa")

            for sub in range(NSUB):
                for c in range(NCD):
                    p.do("tensor", lambda E, c=c, sub=sub: E.matmul(
                        P1[0:96, 0:512], wxb[:, c, :], XLN[:, c, sub * 512:(sub + 1) * 512],
                        start=(c == 0), stop=(c == NCD - 1)),
                        waits=[("dwa", nwa), ("dve", max(xln_done, p.c("dve"))),
                               ("act", p.c("act"))] if c == 0 else (),
                        inc="pe" if c == NCD - 1 else None)
                npe = p.c("pe")
                p.do("scalar", lambda E, sub=sub: E.activation(
                    out=a_xzbc[0:32, sub * 512:(sub + 1) * 512], in_=P1[64:96, 0:512],
                    func=AF.Copy), waits=[("pe", npe)], inc="act")
                p.do("scalar", lambda E, sub=sub: E.activation(
                    out=a_xzb[0:64, sub * 512:(sub + 1) * 512], in_=P1[0:64, 0:512],
                    func=AF.Copy), inc="act")
            nxz_act = p.c("act")

            tm_pool = [0, 0]
            for grp in range(8):
                blks = [2 * grp, 2 * grp + 1]
                for i, blk in enumerate(blks):
                    for sub in range(NSUB):
                        p.do("tensor", lambda E, blk=blk, sub=sub, i=i: E.matmul(
                            PS6[:, 2 * i + sub, :], wdtb_a[0:64, blk * 128:(blk + 1) * 128],
                            a_xzb[0:64, sub * 512:(sub + 1) * 512],
                            start=True, stop=True),
                            waits=[("dwa", nwa), ("act", max(nxz_act, p.c("act")))],
                            inc="pe")
                        npe = p.c("pe")
                        bias = bdt_s[:, blk:blk + 1]
                        ssl = slice(sub * 512, (sub + 1) * 512)
                        bank = 2 * i + sub
                        p.do("scalar", lambda E, bias=bias, ssl=ssl, bank=bank: E.activation(
                            out=a_sp[:, 0, ssl], in_=PS6[:, bank, :], func=AF.Abs,
                            bias=bias), waits=[("pe", npe), ("dve", p.c("dve"))],
                            inc="act")
                        p.do("scalar", lambda E, ssl=ssl: E.activation(
                            out=a_sp[:, 0, ssl], in_=a_sp[:, 0, ssl],
                            func=AF.Exp, scale=-1.0), inc="act")
                        p.do("scalar", lambda E, ssl=ssl: E.activation(
                            out=a_sp[:, 0, ssl], in_=a_sp[:, 0, ssl],
                            func=AF.Ln, bias=1.0), inc="act")
                        p.do("scalar", lambda E, bias=bias, ssl=ssl, bank=bank: E.activation(
                            out=a_sp[:, 1, ssl], in_=PS6[:, bank, :], func=AF.Relu,
                            bias=bias), inc="act")
                        nact = p.c("act")
                        p.do("vector", lambda E, i=i, ssl=ssl: E.tensor_add(
                            a_dl[:, i, ssl], a_sp[:, 0, ssl], a_sp[:, 1, ssl]),
                            waits=[("act", nact)], inc="dve")
                    p.do("vector", lambda E, i=i, blk=blk: E.tensor_mul(
                        a_dx[:, i, :], a_dl[:, i, :], XLN[:, blk, :]), inc="dve")
                ndl = p.c("dve")

                bstart = p.c("dve")
                for s in range(DST):
                    npeB = None
                    p.do("tensor", lambda E, s=s: E.matmul(
                        PS6[:, 2, :], sel32b[:, s, :], a_xzbc[0:32, 0:512],
                        start=True, stop=True),
                        waits=[("act", nxz_act), ("dve", p.c("dve")),
                               ("pool", p.c("pool"))], inc="pe")
                    p.do("tensor", lambda E, s=s: E.matmul(
                        PS6[:, 3, :], sel32b[:, s, :], a_xzbc[0:32, 512:1024],
                        start=True, stop=True), inc="pe")
                    p.do("tensor", lambda E, s=s: E.matmul(
                        PS6[:, 4, :], sel32b[:, 16 + s, :], a_xzbc[0:32, 0:512],
                        start=True, stop=True), inc="pe")
                    p.do("tensor", lambda E, s=s: E.matmul(
                        PS6[:, 5, :], sel32b[:, 16 + s, :], a_xzbc[0:32, 512:1024],
                        start=True, stop=True), inc="pe")
                    npe_bc = p.c("pe")

                    for i, blk in enumerate(blks):
                        p.do("scalar", lambda E, i=i, blk=blk, s=s: E.activation(
                            out=a_ab[:], in_=a_dl[:, i, :], func=AF.Exp,
                            scale=acols[:, blk, s:s + 1]),
                            waits=[("dve", p.c("dve"))], inc="act")
                        nab = p.c("act")
                        p.do("vector", lambda E, i=i: E.tensor_mul(
                            a_bx[:, 0:512], a_dx[:, i, 0:512], PS6[:, 2, :]),
                            waits=[("pe", npe_bc)], inc="dve")
                        p.do("vector", lambda E, i=i: E.tensor_mul(
                            a_bx[:, 512:1024], a_dx[:, i, 512:1024], PS6[:, 3, :]),
                            inc="dve")
                        p.do("vector", lambda E, blk=blk, s=s: E.tensor_tensor_scan(
                            out=a_hs[:], data0=a_ab[:], data1=a_bx[:],
                            initial=h_ssm[:, blk, s:s + 1],
                            op0=ALU.mult, op1=ALU.add),
                            waits=[("act", nab)], inc="dve")
                        slot = (s * 2 + i) % 2
                        if s == 0:
                            p.do("vector", lambda E, i=i: E.tensor_mul(
                                a_y[:, i, 0:512], a_hs[:, 0:512], PS6[:, 4, :]),
                                inc="dve")
                            p.do("vector", lambda E, i=i: E.tensor_mul(
                                a_y[:, i, 512:1024], a_hs[:, 512:1024], PS6[:, 5, :]),
                                inc="dve")
                        else:
                            p.do("vector", lambda E, slot=slot: E.tensor_mul(
                                a_tm[:, slot, 0:512], a_hs[:, 0:512], PS6[:, 4, :]),
                                waits=[("pool", tm_pool[slot])], inc="dve")
                            p.do("vector", lambda E, slot=slot: E.tensor_mul(
                                a_tm[:, slot, 512:1024], a_hs[:, 512:1024], PS6[:, 5, :]),
                                inc="dve")
                            ntm = p.c("dve")
                            tm_pool[slot] = p.do("gpsimd", lambda E, i=i, slot=slot: E.tensor_add(
                                a_y[:, i, :], a_y[:, i, :], a_tm[:, slot, :]),
                                waits=[("dve", ntm)], inc="pool")
                        p.do("vector", lambda E, blk=blk, s=s: E.tensor_copy(
                            out=h_ssm[:, blk, s:s + 1], in_=a_hs[:, T_TILE - 1:T_TILE]),
                            inc="dve")
                npool_r = p.c("pool")
                for i, blk in enumerate(blks):
                    p.do("vector", lambda E, blk=blk, i=i: E.tensor_scalar(
                        out=a_tm[:, i, :], in0=XLN[:, blk, :],
                        scalar1=dvec_s[:, blk:blk + 1], scalar2=None, op0=ALU.mult),
                        waits=[("pool", npool_r)], inc="dve")
                    p.do("vector", lambda E, blk=blk, i=i: E.tensor_add(
                        X[:, blk, :], X[:, blk, :], a_tm[:, i, :]), inc="dve")
                    p.do("vector", lambda E, blk=blk, i=i: E.tensor_add(
                        X[:, blk, :], X[:, blk, :], a_y[:, i, :]), inc="dve")
            return p.c("dve")

        HGS = [(0, 3), (3, 3), (6, 3), (9, 2)]
        DGS = [(0, 3), (3, 3), (6, 3), (9, 3), (12, 3), (15, 1)]

        def ffn_phase(xln_done):
            f_h = av(0, (NHB, T_TILE), BF16)
            f_w13 = av(22 * 1024, (2, 2, 4, 384), BF16)
            f_w2 = av(34 * 1024, (2, 11, 384), BF16)
            f_sa = av(51 * 1024, (6, 512), BF16)
            f_cb = av(57 * 1024, (2, 512), BF16)
            f_sc = av(58 * 1024, (8, 8), F32)
            f_cm = av(58 * 1024 + 512, (8, 8), F32)
            f_mx = av(59 * 1024, (8, 16), F32)
            f_cmb = av(60 * 1024, (2, 128), BF16)
            f_fmb = av(72 * 1024, (T_TILE,), BF16)

            for ch in range(8):
                for c in range(NCD):
                    p.do("tensor", lambda E, ch=ch, c=c: E.matmul(
                        P1[:, 0:8], XLN[:, c, ch * 128:(ch + 1) * 128],
                        gateTb[:, c, :], start=(c == 0), stop=(c == NCD - 1)),
                        waits=[("dve", max(xln_done, p.c("dve"))), ("act", p.c("act")),
                               ("dcon", PRO_CON)] if c == 0 else (),
                        inc="pe" if c == NCD - 1 else None)
                npe_g = p.c("pe")
                p.do("vector", lambda E, ch=ch: E.tensor_copy(
                    out=f_sc[:, ch, :], in_=P1[:, 0:8]),
                    waits=[("pe", npe_g)], inc="dve")
            for ch in range(8):
                p.do("vector", lambda E, ch=ch: E.reduce_max(
                    out=f_mx[:, ch, 0:1], in_=f_sc[:, ch, :],
                    axis=mybir.AxisListType.X), inc="dve")
            for ch in range(8):
                p.do("vector", lambda E, ch=ch: E.tensor_scalar(
                    out=f_sc[:, ch, :], in0=f_sc[:, ch, :], scalar1=f_mx[:, ch, 0:1],
                    scalar2=None, op0=ALU.subtract), inc="dve")
            ndve_sm = p.c("dve")
            for ch in range(8):
                p.do("scalar", lambda E, ch=ch: E.activation(
                    out=f_sc[:, ch, :], in_=f_sc[:, ch, :], func=AF.Exp),
                    waits=[("dve", ndve_sm)] if ch == 0 else (), inc="act")
            nact_sm = p.c("act")
            for ch in range(8):
                p.do("vector", lambda E, ch=ch: E.reduce_sum(
                    out=f_mx[:, ch, 1:2], in_=f_sc[:, ch, :],
                    axis=mybir.AxisListType.X),
                    waits=[("act", nact_sm)] if ch == 0 else (), inc="dve")
            for ch in range(8):
                p.do("vector", lambda E, ch=ch: E.reciprocal(
                    out=f_mx[:, ch, 2:3], in_=f_mx[:, ch, 1:2]), inc="dve")
            for ch in range(8):
                p.do("vector", lambda E, ch=ch: E.tensor_scalar(
                    out=f_sc[:, ch, :], in0=f_sc[:, ch, :], scalar1=f_mx[:, ch, 2:3],
                    scalar2=None, op0=ALU.mult), inc="dve")
            for ch in range(8):
                p.do("vector", lambda E, ch=ch: E.max(
                    out=f_mx[:, ch, 8:16], in_=f_sc[:, ch, :]), inc="dve")
            for ch in range(8):
                p.do("vector", lambda E, ch=ch: E.tensor_tensor(
                    out=f_cm[:, ch, :], in0=f_sc[:, ch, :],
                    in1=f_mx[:, ch, 9:10].to_broadcast([128, 8]), op=ALU.is_ge),
                    inc="dve")
            for ch in range(8):
                p.do("vector", lambda E, ch=ch: E.tensor_mul(
                    f_cm[:, ch, :], f_sc[:, ch, :], f_cm[:, ch, :]), inc="dve")
            for ch in range(8):
                p.do("vector", lambda E, ch=ch: E.tensor_copy(
                    out=f_cmb[:, ch % 2, 0:8], in_=f_cm[:, ch, :]),
                    waits=[("dcb", max(0, p.c("dcb") - 16))], inc="dve")
                ncm = p.c("dve")
                p.do("sync", lambda E, ch=ch: E.dma_start(
                    out=f_fmb[:, ch * 128:(ch + 1) * 128], in_=f_cmb[:, ch % 2, :],
                    transpose=True),
                    waits=[("dve", ncm)], inc="dcb", amt=16)
            if DEBUG_DUMP:
                dbg_sc = av(61 * 1024, (8, 8), F32)
                dbg_cm = av(61 * 1024 + 512, (8, 8), F32)
                dbg_mx = av(62 * 1024, (8, 16), F32)
                p.do("vector", lambda E: E.tensor_copy(
                    out=dbg_sc[:, :, :].rearrange("p a b -> p (a b)"),
                    in_=f_sc[:, :, :].rearrange("p a b -> p (a b)")), inc="dve")
                p.do("vector", lambda E: E.tensor_copy(
                    out=dbg_cm[:, :, :].rearrange("p a b -> p (a b)"),
                    in_=f_cm[:, :, :].rearrange("p a b -> p (a b)")), inc="dve")
                p.do("vector", lambda E: E.tensor_copy(
                    out=dbg_mx[:, :, :].rearrange("p a b -> p (a b)"),
                    in_=f_mx[:, :, :].rearrange("p a b -> p (a b)")), inc="dve")
            ncomb = 0
            ncb_dma = p.c("dcb")

            def dma_w13(e, w, hg_i, q, slot):
                hb0, hbn = HGS[hg_i]

                def fn(E, e=e, w=w, hb0=hb0, hbn=hbn, q=q, slot=slot):
                    if e < NE:
                        srcv = (w1 if w == 0 else w3)[e]
                    else:
                        srcv = sw1 if w == 0 else sw3
                    return E.dma_start(
                        out=f_w13[:, w, slot, :, 0:hbn * 128],
                        in_=srcv[q * 512:(q + 1) * 512,
                                 hb0 * 128:(hb0 + hbn) * 128].rearrange(
                            "(c p) h -> p c h", p=128))
                return fn

            def dma_w2(e, dg_i, slot):
                db0, dbn = DGS[dg_i]

                def fn(E, e=e, db0=db0, dbn=dbn, slot=slot):
                    srcv = w2[e] if e < NE else sw2
                    return E.dma_start(
                        out=f_w2[:, slot, :, 0:dbn * 128],
                        in_=srcv[:, db0 * 128:(db0 + dbn) * 128].rearrange(
                            "(c p) h -> p c h", p=128))
                return fn

            experts = [NE] + list(range(NE))
            w13_free = {(w, s): 0 for w in (0, 1) for s in (0, 1)}
            w2_free = {0: 0, 1: 0}

            for e in experts:
                if e < NE:
                    for sub in range(NSUB):
                        p.do("tensor", lambda E, e=e, sub=sub: E.matmul(
                            P1[:, 0:512], sel8[:, e, :],
                            f_fmb[0:8, sub * 512:(sub + 1) * 512],
                            start=True, stop=True),
                            waits=[("dcb", ncb_dma), ("dve", p.c("dve")),
                                   ("act", p.c("act"))], inc="pe")
                        npe_cb = p.c("pe")
                        p.do("scalar", lambda E, sub=sub: E.activation(
                            out=f_cb[:, sub, :], in_=P1[:, 0:512], func=AF.Copy),
                            waits=[("pe", npe_cb)], inc="act")
                    ncb_act = p.c("act")
                else:
                    ncb_act = p.c("act")

                bank_act = [0] * 6
                bank_free_dve = [0] * 6
                sa_free_dve = [0] * 6

                for hg_i in range(4):
                    hb0, hbn = HGS[hg_i]
                    nj = hbn * 2
                    for wpass in (0, 1):
                        for q in (0, 1):
                            p.do("gpsimd", dma_w13(e, wpass, hg_i, q, q),
                                 waits=[("pe", w13_free[(wpass, q)])],
                                 inc="dw13", amt=16)
                        ndma01 = p.c("dw13")
                        for q in (0, 1):
                            for kc in range(4):
                                for j in range(nj):
                                    hbl, sub = divmod(j, 2)
                                    wts = []
                                    if q == 0 and kc == 0 and j == 0:
                                        wts = [("dw13", ndma01),
                                               ("dve", max(xln_done, p.c("dve"))),
                                               ("act", max(ncb_act, max(bank_act)))]
                                    last = (q == 1 and kc == 3 and j == nj - 1)
                                    p.do("tensor", lambda E, wpass=wpass, q=q,
                                         kc=kc, j=j, hbl=hbl, sub=sub: E.matmul(
                                             PS6[:, j, :],
                                             f_w13[:, wpass, q, kc,
                                                   hbl * 128:(hbl + 1) * 128],
                                             XLN[:, q * 4 + kc,
                                                 sub * 512:(sub + 1) * 512],
                                             start=(q == 0 and kc == 0),
                                             stop=False),
                                         waits=wts, inc="pe" if last else None)
                        npe_p1 = p.c("pe")
                        for q in (2, 3):
                            p.do("gpsimd", dma_w13(e, wpass, hg_i, q, q - 2),
                                 waits=[("pe", npe_p1)], inc="dw13", amt=16)
                        ndma23 = p.c("dw13")
                        for j in range(nj):
                            hbl, sub = divmod(j, 2)
                            for q in (2, 3):
                                for kc in range(4):
                                    wts = []
                                    if j == 0 and q == 2 and kc == 0:
                                        wts = [("dw13", ndma23)]
                                    p.do("tensor", lambda E, wpass=wpass, q=q,
                                         kc=kc, j=j, hbl=hbl, sub=sub: E.matmul(
                                             PS6[:, j, :],
                                             f_w13[:, wpass, q - 2, kc,
                                                   hbl * 128:(hbl + 1) * 128],
                                             XLN[:, q * 4 + kc,
                                                 sub * 512:(sub + 1) * 512],
                                             start=False,
                                             stop=(q == 3 and kc == 3)),
                                         waits=wts,
                                         inc="pe" if (q == 3 and kc == 3) else None)
                            npe_j = p.c("pe")
                            if wpass == 0:
                                bank_act[j] = p.do("scalar", lambda E, j=j: E.activation(
                                    out=f_sa[:, j, :], in_=PS6[:, j, :], func=AF.Silu),
                                    waits=[("pe", npe_j), ("dve", sa_free_dve[j])],
                                    inc="act")
                            else:
                                hcol = slice(sub * 512, (sub + 1) * 512)
                                p.do("vector", lambda E, j=j, hbl=hbl, hb0=hb0,
                                     hcol=hcol: E.tensor_mul(
                                         f_h[:, hb0 + hbl, hcol], f_sa[:, j, :],
                                         PS6[:, j, :]),
                                    waits=[("pe", npe_j), ("act", bank_act[j]),
                                           ("pool", p.c("pool"))], inc="dve")
                                sa_free_dve[j] = p.c("dve")
                                bank_free_dve[j] = p.c("dve")
                                if e < NE:
                                    p.do("gpsimd", lambda E, j=j, hbl=hbl, hb0=hb0,
                                         hcol=hcol, sub=sub: E.tensor_mul(
                                             f_h[:, hb0 + hbl, hcol],
                                             f_h[:, hb0 + hbl, hcol], f_cb[:, sub, :]),
                                        waits=[("dve", p.c("dve"))], inc="pool")
                        w13_free[(wpass, 0)] = p.c("pe")
                        w13_free[(wpass, 1)] = p.c("pe")
                n_hdone_d = p.c("dve")
                n_hdone_p = p.c("pool")

                xadd_dve = {0: 0, 1: 0}
                for dg_i in range(6):
                    p.do("gpsimd", dma_w2(e, dg_i, dg_i % 2),
                         waits=[("pe", w2_free[dg_i % 2])], inc="dw2", amt=16)
                    ndw2 = p.c("dw2")
                    db0, dbn = DGS[dg_i]
                    for sub in range(NSUB):
                        trio = (dg_i * NSUB + sub) % 2
                        banks = [3 * trio + t for t in range(dbn)]
                        for k in range(NHB):
                            for t in range(dbn):
                                wts = []
                                if k == 0 and t == 0:
                                    wts = [("dw2", ndw2),
                                           ("dve", max(n_hdone_d, xadd_dve[trio],
                                                       max(bank_free_dve))),
                                           ("pool", n_hdone_p),
                                           ("act", max(bank_act))]
                                p.do("tensor", lambda E, k=k, t=t, sub=sub,
                                     banks=banks, dg_i=dg_i: E.matmul(
                                         PS6[:, banks[t], :],
                                         f_w2[:, dg_i % 2, k, t * 128:(t + 1) * 128],
                                         f_h[:, k, sub * 512:(sub + 1) * 512],
                                         start=(k == 0), stop=(k == NHB - 1)),
                                     waits=wts,
                                     inc="pe" if (k == NHB - 1 and t == dbn - 1) else None)
                        npe_u = p.c("pe")
                        for t in range(dbn):
                            p.do("vector", lambda E, t=t, sub=sub, banks=banks,
                                 db0=db0: E.tensor_add(
                                     X[:, db0 + t, sub * 512:(sub + 1) * 512],
                                     X[:, db0 + t, sub * 512:(sub + 1) * 512],
                                     PS6[:, banks[t], :]),
                                waits=[("pe", npe_u)], inc="dve")
                        xadd_dve[trio] = p.c("dve")
                    w2_free[dg_i % 2] = p.c("pe")
                for w in (0, 1):
                    for s_ in (0, 1):
                        w13_free[(w, s_)] = p.c("pe")
                bank_free_dve = [p.c("dve")] * 6
            return p.c("dve")

        def attn_b_phase(xln_done):
            if B_STAGE < 0:
                return p.c("dve")
            b_fq = av(0, (32, 520), BF16)
            b_ctx = av(41 * 1024, (16, 512), BF16)
            b_ltm = av(57 * 1024, (8, 16), F32)
            b_btm = av(57 * 1024 + 512, (8, 16), F32)
            b_cols = av(58 * 1024, (16,), F32)
            b_lam = [av(58 * 1024 + 64, (16,), F32), av(58 * 1024 + 128, (16,), F32)]
            b_khat = av(60 * 1024, (8, 128), BF16)
            b_qhat = av(62 * 1024, (8, 128), BF16)
            b_row_beta = av(LN_ROWS_OFF, (T_TILE,), F32)
            b_row_L = av(LN_ROWS_OFF + 4096, (T_TILE,), F32)
            b_row_Lam = av(LN_ROWS_OFF + 8192, (T_TILE,), F32)
            b_ones16 = av(44 * 1024, (T_TILE,), F32)
            s_ksq = av(33 * 1024, (8, 128), BF16)
            s_qsq = av(35 * 1024, (8, 128), BF16)

            def bscr_bf(i):
                return av(33 * 1024 + i * 256, (128,), BF16)

            def bscr_f32(i):
                return av(37 * 1024 + i * 512, (128,), F32)

            s_E, s_E2, s_P, s_PT = bscr_bf(0), bscr_bf(1), bscr_bf(2), bscr_bf(3)
            s_X = [bscr_bf(4), bscr_bf(5)]
            s_XT = [bscr_bf(6), bscr_bf(7)]
            s_U, s_G, s_W, s_Wm = bscr_bf(8), bscr_bf(9), bscr_bf(10), bscr_bf(11)
            s_kv, s_Vt, s_on, s_Osq = bscr_bf(12), bscr_bf(13), bscr_bf(14), bscr_bf(15)
            s_Kt = av(39 * 1024, (128,), BF16)
            s_Ktm = av(39 * 1024 + 256, (128,), BF16)
            s_neg, s_t1, s_O, s_t3 = (bscr_f32(0), bscr_f32(1), bscr_f32(2),
                                      bscr_f32(3))
            rrA = av(LN_ROWS_OFF, (512,), F32)
            rrB = av(LN_ROWS_OFF + 2048, (512,), F32)
            rk_row = av(LN_ROWS_OFF + 4096, (T_TILE,), BF16)
            rq_row = av(LN_ROWS_OFF + 6144, (T_TILE,), BF16)
            pr_ring = av(33 * 1024, (2, 8, 384), BF16)
            pr_raw = av(45 * 1024, (2, 516), BF16)
            pr_acc = av(47 * 1024, (2, 512), F32)
            g_tmp = av(16 * 1024, (512,), BF16)
            gated = av(0, (16, 512), BF16)
            g_ring = av(33 * 1024, (2, 8, 256), BF16)

            for (row_out, wsrc, isbeta) in ((b_row_beta, wbetab, True),
                                            (b_row_L, walphab, False)):
                for sub in range(NSUB):
                    for c in range(NCD):
                        p.do("tensor", lambda E, c=c, sub=sub, wsrc=wsrc: E.matmul(
                            PS6[0:16, 0, :], wsrc[:, c, :],
                            XLN[:, c, sub * 512:(sub + 1) * 512],
                            start=(c == 0), stop=(c == NCD - 1)),
                            waits=[("dve", max(xln_done, p.c("dve"))),
                                   ("act", p.c("act")), ("dcon", PRO_CON)]
                            if c == 0 else (),
                            inc="pe" if c == NCD - 1 else None)
                    npe_r = p.c("pe")
                    rsl = slice(sub * 512, (sub + 1) * 512)
                    if isbeta:
                        p.do("scalar", lambda E, rsl=rsl, row_out=row_out: E.activation(
                            out=row_out[0:16, rsl], in_=PS6[0:16, 0, :],
                            func=AF.Sigmoid), waits=[("pe", npe_r)], inc="act")
                    else:
                        p.do("scalar", lambda E, rsl=rsl, row_out=row_out: E.activation(
                            out=row_out[0:16, rsl], in_=PS6[0:16, 0, :],
                            func=AF.Abs, bias=dtb_s[:]),
                            waits=[("pe", npe_r)], inc="act")
                        p.do("scalar", lambda E, rsl=rsl, row_out=row_out: E.activation(
                            out=row_out[0:16, rsl], in_=row_out[0:16, rsl],
                            func=AF.Exp, scale=-1.0), inc="act")
                        p.do("scalar", lambda E, rsl=rsl, row_out=row_out: E.activation(
                            out=row_out[0:16, rsl], in_=row_out[0:16, rsl],
                            func=AF.Ln, bias=1.0), inc="act")
                        p.do("scalar", lambda E, rsl=rsl: E.activation(
                            out=b_row_Lam[0:16, rsl], in_=PS6[0:16, 0, :],
                            func=AF.Relu, bias=dtb_s[:]), inc="act")
                        nact_sp = p.c("act")
                        p.do("vector", lambda E, rsl=rsl, row_out=row_out: E.tensor_add(
                            row_out[0:16, rsl], row_out[0:16, rsl],
                            b_row_Lam[0:16, rsl]),
                            waits=[("act", nact_sp)], inc="dve")
                        p.do("vector", lambda E, rsl=rsl, row_out=row_out:
                             E.tensor_scalar(
                                 out=row_out[0:16, rsl], in0=row_out[0:16, rsl],
                                 scalar1=negA_col[:], scalar2=None, op0=ALU.mult),
                             inc="dve")
            nrow_d = p.c("dve")
            nrow_a = p.c("act")
            p.do("gpsimd", lambda E: E.memset(b_ones16[0:16, :], 1.0),
                 waits=[("dve", max(xln_done, p.c("dve")))], inc="pool")
            p.do("gpsimd", lambda E: E.memset(b_lam[0][:], 0.0), inc="pool")
            p.do("gpsimd", lambda E: E.memset(b_lam[1][:], 0.0), inc="pool")
            npl_ones = p.c("pool")
            p.do("vector", lambda E: E.tensor_tensor_scan(
                out=b_row_Lam[0:16, :], data0=b_ones16[0:16, :],
                data1=b_row_L[0:16, :], initial=0.0, op0=ALU.mult, op1=ALU.add),
                waits=[("pool", npl_ones), ("act", nrow_a), ("dve", nrow_d)],
                inc="dve")
            nlam = p.c("dve")
            for cg in range(8):
                p.do("tensor", lambda E, cg=cg: E.transpose(
                    PS6[:, 1, 0:16], b_row_Lam[0:16, cg * 128:(cg + 1) * 128],
                    ident_f[0:16, 0:16]),
                    waits=[("dve", max(nlam, p.c("dve")))], inc="pe")
                npe_t = p.c("pe")
                p.do("vector", lambda E, cg=cg: E.tensor_copy(
                    out=b_ltm[:, cg, :], in_=PS6[:, 1, 0:16]),
                    waits=[("pe", npe_t)], inc="dve")
                p.do("tensor", lambda E, cg=cg: E.transpose(
                    PS6[:, 2, 0:16], b_row_beta[0:16, cg * 128:(cg + 1) * 128],
                    ident_f[0:16, 0:16]),
                    waits=([("act", nrow_a)] if cg == 0 else ()), inc="pe")
                npe_t2 = p.c("pe")
                p.do("vector", lambda E, cg=cg: E.tensor_copy(
                    out=b_btm[:, cg, :], in_=PS6[:, 2, 0:16]),
                    waits=[("pe", npe_t2)], inc="dve")

            psb = {"w": 0, "rd": 0, "ra": 0}

            def psb_write(fn, extra=()):
                p.do("tensor", fn,
                     waits=list(extra) + [("dve", p.c("dve")), ("act", p.c("act"))],
                     inc="pe")
                psb["w"] = p.c("pe")
                return psb["w"]

            bkr = {b: [0, 0] for b in range(6)}

            def bw(b):
                return [("dve", bkr[b][0]), ("act", bkr[b][1])]

            if B_STAGE < 1:
                return p.c("dve")
            for sub in range(NSUB):
                scol = slice(sub * 512, (sub + 1) * 512)
                OBGS = [(3 * i, 3) for i in range(10)] + [(30, 2)]
                ring_free = p.c("pe")
                rdve, ract = p.c("dve"), p.c("act")
                for gi, (ob0, obn) in enumerate(OBGS):
                    trio = gi % 2
                    for half in range(2):
                        extra = ([("dve", rdve), ("act", ract)]
                                 if gi == 0 else [])
                        p.do("gpsimd", lambda E, ob0=ob0, obn=obn, half=half: E.dma_start(
                            out=pr_ring[:, half, :, 0:obn * 128],
                            in_=wqkvT[half * 1024:(half + 1) * 1024,
                                      ob0 * 128:(ob0 + obn) * 128].rearrange(
                                "(c p) h -> p c h", p=128)),
                            waits=[("pe", ring_free)] + extra, inc="dwq", amt=16)
                    ndq = p.c("dwq")
                    for half in range(2):
                        for kc in range(8):
                            for obl in range(obn):
                                wts = []
                                if half == 0 and kc == 0 and obl == 0:
                                    wts = ([("dwq", ndq),
                                            ("dve", max(xln_done, p.c("dve"))),
                                            ("act", p.c("act"))]
                                           + bw(3 * trio) + bw(3 * trio + 1)
                                           + bw(3 * trio + 2))
                                p.do("tensor", lambda E, kc=kc, obl=obl, half=half,
                                     trio=trio, scol=scol: E.matmul(
                                         PS6[:, 3 * trio + obl, :],
                                         pr_ring[:, half, kc, obl * 128:(obl + 1) * 128],
                                         XLN[:, half * 8 + kc, scol],
                                         start=(half == 0 and kc == 0),
                                         stop=(half == 1 and kc == 7)),
                                     waits=wts,
                                     inc="pe" if (half == 1 and kc == 7
                                                  and obl == obn - 1) else None)
                    npe_grp = p.c("pe")
                    ring_free = npe_grp
                    for obl in range(obn):
                        ob = ob0 + obl
                        rs = ob % 2
                        p.do("vector", lambda E, ob=ob, rs=rs: E.tensor_copy(
                            out=pr_raw[:, rs, 0:3], in_=conv_tail[:, ob, :]),
                            waits=[("act", p.c("act"))], inc="dve")
                        p.do("scalar", lambda E, rs=rs, trio=trio, obl=obl: E.activation(
                            out=pr_raw[:, rs, 3:515], in_=PS6[:, 3 * trio + obl, :],
                            func=AF.Copy),
                            waits=[("pe", npe_grp), ("dve", p.c("dve"))], inc="act")
                        bkr[3 * trio + obl][1] = p.c("act")
                        nact_raw = p.c("act")
                        p.do("vector", lambda E, ob=ob, rs=rs: E.tensor_scalar(
                            out=pr_acc[:, rs, :], in0=pr_raw[:, rs, 0:512],
                            scalar1=convw_s[:, ob, 0:1], scalar2=None, op0=ALU.mult),
                            waits=[("act", nact_raw)], inc="dve")
                        for tau in (1, 2, 3):
                            p.do("vector", lambda E, ob=ob, rs=rs, tau=tau:
                                 E.scalar_tensor_tensor(
                                     out=pr_acc[:, rs, :],
                                     in0=pr_raw[:, rs, tau:tau + 512],
                                     scalar=convw_s[:, ob, tau:tau + 1],
                                     in1=pr_acc[:, rs, :], op0=ALU.mult, op1=ALU.add),
                                 inc="dve")
                        p.do("vector", lambda E, ob=ob, rs=rs: E.tensor_copy(
                            out=conv_tail[:, ob, :], in_=pr_raw[:, rs, 512:515]),
                            inc="dve")
                        ndve_cv = p.c("dve")
                        p.do("scalar", lambda E, ob=ob, rs=rs: E.activation(
                            out=b_fq[:, ob, 3:515], in_=pr_acc[:, rs, :],
                            func=AF.Silu), waits=[("dve", ndve_cv)], inc="act")
                nproj_a = p.c("act")

                for c4 in range(4 if B_STAGE >= 2 else 0):
                    cg = sub * 4 + c4
                    ccol = slice(3 + c4 * 128, 3 + (c4 + 1) * 128)
                    cxcol = slice(c4 * 128, (c4 + 1) * 128)
                    lamP = b_lam[cg % 2]
                    lamN = b_lam[(cg + 1) % 2]
                    lamcols = slice(cg * 128, (cg + 1) * 128)
                    p.do("scalar", lambda E, ccol=ccol: E.activation(
                        out=s_ksq[:, :, :],
                        in_=b_fq[:, 8:16, ccol],
                        func=AF.Square),
                        waits=[("act", nproj_a), ("dve", p.c("dve")),
                               ("pe", p.c("pe"))], inc="act")
                    p.do("scalar", lambda E, ccol=ccol: E.activation(
                        out=s_qsq[:, :, :],
                        in_=b_fq[:, 0:8, ccol],
                        func=AF.Square), inc="act")
                    nsq_a = p.c("act")
                    for i, (srcv, bank) in enumerate(((s_ksq, 0), (s_ksq, 1),
                                                     (s_qsq, 2), (s_qsq, 3))):
                        piece = i % 2
                        p.do("tensor", lambda E, srcv=srcv, bank=bank, piece=piece:
                             E.matmul(
                                 PS6[0:1, bank, :], onescol_b[:],
                                 srcv[:, :, :].rearrange("p a b -> p (a b)")[
                                     :, piece * 512:(piece + 1) * 512],
                                 start=True, stop=True),
                             waits=([("act", nsq_a)] + bw(0) + bw(1) + bw(2) + bw(3))
                             if i == 0 else (),
                             inc="pe")
                    npe_ssq = p.c("pe")
                    specs = ((0, rk_row, False, 0), (1, rk_row, False, 1),
                             (2, rq_row, True, 0), (3, rq_row, True, 1))
                    for pair in (specs[0:2], specs[2:4]):
                        for i, (bank, outrow, qf, piece) in enumerate(pair):
                            rr = rrA if piece == 0 else rrB
                            p.do("scalar", lambda E, bank=bank, rr=rr: E.activation(
                                out=rr[0:1, :], in_=PS6[0:1, bank, :], func=AF.Ln,
                                bias=fc[0:1, 2:3]),
                                waits=[("pe", npe_ssq)] if bank == 0 else (),
                                inc="act")
                        for i, (bank, outrow, qf, piece) in enumerate(pair):
                            rr = rrA if piece == 0 else rrB
                            if qf:
                                p.do("scalar", lambda E, rr=rr, outrow=outrow,
                                     piece=piece: E.activation(
                                         out=outrow[0:1, piece * 512:(piece + 1) * 512],
                                         in_=rr[0:1, :], func=AF.Exp,
                                         scale=fc[0:1, 1:2], bias=fc[0:1, 3:4]),
                                     inc="act")
                            else:
                                p.do("scalar", lambda E, rr=rr, outrow=outrow,
                                     piece=piece: E.activation(
                                         out=outrow[0:1, piece * 512:(piece + 1) * 512],
                                         in_=rr[0:1, :], func=AF.Exp,
                                         scale=fc[0:1, 1:2]),
                                     inc="act")
                    nrow2_a = p.c("act")
                    for i, (row, bank) in enumerate(((rk_row, 0), (rk_row, 1),
                                                    (rq_row, 2), (rq_row, 3))):
                        piece = i % 2
                        p.do("tensor", lambda E, row=row, bank=bank, piece=piece:
                             E.matmul(
                                 PS6[:, bank, :], onesb1[:],
                                 row[0:1, piece * 512:(piece + 1) * 512],
                                 start=True, stop=True),
                             waits=[("act", nrow2_a)] if i == 0 else (), inc="pe")
                    npe_bc = p.c("pe")
                    p.do("vector", lambda E, ccol=ccol: E.tensor_mul(
                        b_khat[:, :, :],
                        b_fq[:, 8:16, ccol],
                        PS6[:, 0:2, :].rearrange("p a (c d) -> p (a c) d", c=4, d=128)),
                        waits=[("pe", npe_bc)], inc="dve")
                    p.do("vector", lambda E, ccol=ccol: E.tensor_mul(
                        b_qhat[:, :, :],
                        b_fq[:, 0:8, ccol],
                        PS6[:, 2:4, :].rearrange("p a (c d) -> p (a c) d", c=4, d=128)),
                        inc="dve")
                    nhat_d = p.c("dve")
                    for b in range(4):
                        bkr[b][0] = nhat_d

                    for h in range(NH if C_STAGE >= 1 else 0):
                        hq = h // 2
                        Lcol = b_ltm[:, cg, h:h + 1]
                        Bcol = b_btm[:, cg, h:h + 1]
                        if h % 2 == 0:
                            p.do("tensor", lambda E, hq=hq: E.matmul(
                                PS6[:, 0, 0:128], b_khat[:, hq, :], b_khat[:, hq, :],
                                start=True, stop=True),
                                waits=bw(0) + [("dve", p.c("dve")), ("act", p.c("act"))], inc="pe")
                            npe_G = p.c("pe")
                            p.do("tensor", lambda E, hq=hq: E.matmul(
                                PS6[:, 1, 0:128], b_khat[:, hq, :], b_qhat[:, hq, :],
                                start=True, stop=True), waits=bw(1) + [("dve", p.c("dve")), ("act", p.c("act"))], inc="pe")
                            npe_W = p.c("pe")
                            psb_write(lambda E, hq=hq: E.transpose(
                                PSB[:, hq % 2, :], b_khat[:, hq, :], ident_b[:]))
                            npe_ktm = psb["w"]
                            p.do("vector", lambda E, hq=hq: E.tensor_copy(
                                out=s_Ktm[:], in_=PSB[:, hq % 2, :]),
                                waits=[("pe", psb["w"])], inc="dve")
                            psb["rd"] = p.c("dve")
                            p.do("vector", lambda E: E.tensor_copy(
                                out=s_G[:], in_=PS6[:, 0, 0:128]),
                                waits=[("pe", npe_G)], inc="dve")
                            bkr[0][0] = p.c("dve")
                            p.do("scalar", lambda E: E.activation(
                                out=s_W[:], in_=PS6[:, 1, 0:128], func=AF.Copy),
                                waits=[("pe", npe_W)], inc="act")
                            bkr[1][1] = p.c("act")
                        p.do("tensor", lambda E, Lcol=Lcol: E.matmul(
                            PS6[:, 1, 0:1], selrow_last_f[:], Lcol,
                            start=True, stop=True), waits=bw(1) + [("dve", p.c("dve")), ("act", p.c("act"))], inc="pe")
                        npe_le = p.c("pe")
                        p.do("vector", lambda E, Lcol=Lcol: E.tensor_scalar(
                            out=s_neg[:], in0=ones128f[:], scalar1=Lcol,
                            scalar2=None, op0=ALU.mult),
                            waits=[("dcon", PRO_CON)], inc="dve")
                        ndve_t1 = p.c("dve")
                        p.do("tensor", lambda E: E.transpose(
                            PS6[:, 2, 0:128], s_neg[:], ident_f[:]),
                            waits=bw(2) + [("dve", p.c("dve")), ("act", p.c("act"))], inc="pe")
                        npe_m1 = p.c("pe")
                        psb_write(lambda E, h=h, ccol=ccol: E.transpose(
                            PSB[:, 2 + h % 2, :], b_fq[:, 16 + h, ccol], ident_b[:]))
                        npe_vt = psb["w"]
                        p.do("tensor", lambda E, h=h, hq=hq: E.matmul(
                            PS6[:, 3, 0:128], b_khat[:, hq, :], Sb_st[:, h, :],
                            start=True, stop=True),
                            waits=bw(3) + [("dve", p.c("dve")), ("act", p.c("act"))], inc="pe")
                        npe_ks = p.c("pe")
                        p.do("vector", lambda E, Lcol=Lcol, lamP=lamP, h=h:
                             E.tensor_sub(b_cols[:, 0:1], Lcol, lamP[:, h:h + 1]),
                             waits=[("pe", npe_le), ("act", p.c("act"))], inc="dve")
                        p.do("vector", lambda E, Lcol=Lcol: E.tensor_sub(
                            b_cols[:, 1:2], Lcol, PS6[:, 1, 0:1]), inc="dve")
                        p.do("vector", lambda E, lamP=lamP, h=h: E.tensor_sub(
                            b_cols[:, 2:3], PS6[:, 1, 0:1], lamP[:, h:h + 1]),
                            inc="dve")
                        bkr[1][0] = p.c("dve")
                        p.do("scalar", lambda E: E.activation(
                            out=b_cols[:, 4:5], in_=b_cols[:, 0:1], func=AF.Exp),
                            waits=[("dve", p.c("dve"))], inc="act")
                        p.do("scalar", lambda E: E.activation(
                            out=b_cols[:, 5:6], in_=b_cols[:, 1:2], func=AF.Exp,
                            scale=-1.0), inc="act")
                        p.do("scalar", lambda E: E.activation(
                            out=b_cols[:, 6:7], in_=b_cols[:, 2:3], func=AF.Exp),
                            inc="act")
                        nact_cols = p.c("act")
                        p.do("vector", lambda E, Lcol=Lcol: E.tensor_sub(
                            s_neg[:], PS6[:, 2, 0:128],
                            Lcol.to_broadcast([128, 128])),
                            waits=[("pe", npe_m1)], inc="dve")
                        bkr[2][0] = p.c("dve")
                        ndve_neg = p.c("dve")
                        p.do("vector", lambda E: E.tensor_add(
                            s_O[:], s_neg[:], triuBIG[:]), inc="dve")
                        ndve_nm1 = p.c("dve")
                        p.do("scalar", lambda E: E.activation(
                            out=s_E[:], in_=s_O[:], func=AF.Exp, scale=-1.0),
                            waits=[("dve", ndve_nm1)], inc="act")
                        nact_E1 = p.c("act")
                        p.do("vector", lambda E: E.tensor_add(
                            s_O[:], s_neg[:], trilBIG[:]),
                            waits=[("act", nact_E1)], inc="dve")
                        ndve_negm = p.c("dve")
                        p.do("scalar", lambda E: E.activation(
                            out=s_E2[:], in_=s_O[:], func=AF.Exp),
                            waits=[("dve", ndve_negm)], inc="act")
                        nact_E = p.c("act")
                        p.do("vector", lambda E, Bcol=Bcol: E.tensor_mul(
                            b_cols[:, 3:4], Bcol, b_cols[:, 4:5]),
                            waits=[("act", nact_cols)], inc="dve")
                        p.do("vector", lambda E, Bcol=Bcol: E.tensor_scalar(
                            out=s_t1[:], in0=s_G[:], scalar1=Bcol, scalar2=None,
                            op0=ALU.mult), inc="dve")
                        p.do("vector", lambda E: E.tensor_mul(
                            s_t1[:], s_t1[:], s_E[:]),
                            waits=[("act", nact_E)], inc="dve")
                        p.do("vector", lambda E: E.tensor_mul(
                            s_P[:], s_t1[:], trilS_neg[:]), inc="dve")
                        p.do("vector", lambda E: E.tensor_mul(
                            s_Wm[:], s_W[:], s_E2[:]), inc="dve")
                        ndve_P = p.c("dve")
                        psb_write(lambda E: E.transpose(PSB[:, 4, :], s_P[:],
                                                        ident_b[:]),
                                  extra=[("dve", ndve_P)])
                        npe_pt = psb["w"]
                        p.do("vector", lambda E: E.tensor_copy(
                            out=s_PT[:], in_=PSB[:, 4, :]),
                            waits=[("pe", psb["w"])], inc="dve")
                        psb["rd"] = p.c("dve")
                        p.do("vector", lambda E: E.tensor_mul(
                            s_kv[:], PS6[:, 3, 0:128],
                            b_cols[:, 3:4].to_broadcast([128, 128])),
                            waits=[("pe", npe_ks)], inc="dve")
                        bkr[3][0] = p.c("dve")
                        p.do("vector", lambda E, Bcol=Bcol, h=h: E.tensor_mul(
                            s_Vt[:], PSB[:, 2 + h % 2, :],
                            Bcol.to_broadcast([128, 128])),
                            waits=[("pe", psb["w"])], inc="dve")
                        psb["rd"] = p.c("dve")
                        p.do("vector", lambda E: E.tensor_sub(
                            s_U[:], s_Vt[:], s_kv[:]), inc="dve")
                        ndve_u0 = p.c("dve")
                        if C_STAGE < 2:
                            continue
                        for k in range(7):
                            XT_cur = s_XT[k % 2] if k > 0 else s_PT
                            X_cur = s_X[k % 2] if k > 0 else s_P
                            p.do("tensor", lambda E, XT_cur=XT_cur: E.matmul(
                                PS6[:, 5, 0:128], XT_cur, s_U[:],
                                start=True, stop=True),
                                waits=bw(5) + [("dve", p.c("dve")), ("act", p.c("act"))], inc="pe")
                            npe_u = p.c("pe")
                            p.do("vector", lambda E: E.tensor_add(
                                s_U[:], PS6[:, 5, 0:128], s_U[:]),
                                waits=[("pe", npe_u)], inc="dve")
                            bkr[5][0] = p.c("dve")
                            if k < 6:
                                p.do("tensor", lambda E, XT_cur=XT_cur,
                                     X_cur=X_cur: E.matmul(
                                         PS6[:, 0, 0:128], XT_cur, X_cur,
                                         start=True, stop=True),
                                     waits=bw(0) + [("dve", p.c("dve")), ("act", p.c("act"))], inc="pe")
                                npe_x = p.c("pe")
                                p.do("tensor", lambda E, XT_cur=XT_cur,
                                     X_cur=X_cur: E.matmul(
                                         PS6[:, 4, 0:128], X_cur, XT_cur,
                                         start=True, stop=True),
                                     waits=bw(4) + [("dve", p.c("dve")), ("act", p.c("act"))], inc="pe")
                                npe_xt = p.c("pe")
                                p.do("vector", lambda E, k=k: E.tensor_copy(
                                    out=s_X[(k + 1) % 2][:], in_=PS6[:, 0, 0:128]),
                                    waits=[("pe", npe_x)], inc="dve")
                                bkr[0][0] = p.c("dve")
                                p.do("scalar", lambda E, k=k: E.activation(
                                    out=s_XT[(k + 1) % 2][:], in_=PS6[:, 4, 0:128],
                                    func=AF.Copy),
                                    waits=[("pe", npe_xt)], inc="act")
                                bkr[4][1] = p.c("act")
                        ndve_solve = p.c("dve")
                        nact_solve = p.c("act")
                        if C_STAGE < 3:
                            continue
                        p.do("tensor", lambda E: E.matmul(
                            PS6[:, 3, 0:128], s_Wm[:], s_U[:],
                            start=True, stop=True),
                            waits=bw(3) + [("dve", p.c("dve")), ("act", p.c("act"))], inc="pe")
                        npe_o1 = p.c("pe")
                        p.do("tensor", lambda E, h=h, hq=hq: E.matmul(
                            PS6[:, 2, 0:128], b_qhat[:, hq, :], Sb_st[:, h, :],
                            start=True, stop=True),
                            waits=bw(2) + [("dve", p.c("dve")), ("act", p.c("act"))], inc="pe")
                        npe_o2 = p.c("pe")
                        p.do("vector", lambda E: E.tensor_mul(
                            s_t3[:], PS6[:, 2, 0:128],
                            b_cols[:, 4:5].to_broadcast([128, 128])),
                            waits=[("pe", npe_o2)], inc="dve")
                        bkr[2][0] = p.c("dve")
                        p.do("vector", lambda E: E.tensor_add(
                            s_O[:], PS6[:, 3, 0:128], s_t3[:]),
                            waits=[("pe", npe_o1)], inc="dve")
                        bkr[3][0] = p.c("dve")
                        ndve_O = p.c("dve")
                        p.do("scalar", lambda E: E.activation(
                            out=s_Osq[:], in_=s_O[:], func=AF.Square),
                            waits=[("dve", ndve_O)], inc="act")
                        nact_osq = p.c("act")
                        p.do("vector", lambda E: E.reduce_sum(
                            out=b_cols[:, 7:8], in_=s_Osq[:],
                            axis=mybir.AxisListType.X),
                            waits=[("act", nact_osq)], inc="dve")
                        ndve_ssq = p.c("dve")
                        p.do("scalar", lambda E: E.activation(
                            out=b_cols[:, 8:9], in_=b_cols[:, 7:8], func=AF.Ln,
                            scale=fc[:, 4:5], bias=fc[:, 2:3]),
                            waits=[("dve", ndve_ssq)], inc="act")
                        p.do("scalar", lambda E, lamN=lamN, h=h: E.activation(
                            out=lamN[:, h:h + 1], in_=PS6[:, 1, 0:1], func=AF.Copy),
                            inc="act")
                        bkr[1][1] = p.c("act")
                        p.do("scalar", lambda E: E.activation(
                            out=b_cols[:, 9:10], in_=b_cols[:, 8:9], func=AF.Exp,
                            scale=-0.5), inc="act")
                        nact_rstd = p.c("act")
                        p.do("vector", lambda E: E.scalar_tensor_tensor(
                            out=s_on[:], in0=s_O[:], scalar=b_cols[:, 9:10],
                            in1=rms_bc[:], op0=ALU.mult, op1=ALU.mult),
                            waits=[("act", nact_rstd)], inc="dve")
                        ndve_on = p.c("dve")
                        psb_write(lambda E, h=h: E.transpose(
                            PSB[:, 5 + h % 2, :], s_on[:], ident_b[:]),
                            extra=[("dve", ndve_on)])
                        npe_ctx = psb["w"]
                        p.do("scalar", lambda E, h=h, cxcol=cxcol: E.activation(
                            out=b_ctx[:, h, cxcol], in_=PSB[:, 5 + h % 2, :],
                            func=AF.Copy), waits=[("pe", psb["w"])], inc="act")
                        psb["ra"] = p.c("act")
                        if C_STAGE < 31:
                            continue
                        p.do("vector", lambda E: E.tensor_mul(
                            s_Kt[:], s_Ktm[:],
                            b_cols[:, 5:6].to_broadcast([128, 128])),
                            inc="dve")
                        ndve_kt = p.c("dve")
                        if C_STAGE < 32:
                            continue
                        p.do("tensor", lambda E: E.matmul(
                            PS6[:, 2, 0:128], s_Kt[:], s_U[:],
                            start=True, stop=True),
                            waits=bw(2) + [("dve", p.c("dve")), ("act", p.c("act"))], inc="pe")
                        npe_sup = p.c("pe")
                        if C_STAGE < 33:
                            bkr[2][1] = bkr[2][1]
                            p.do("vector", lambda E: E.tensor_copy(
                                out=s_t3[:], in_=PS6[:, 2, 0:128]),
                                waits=[("pe", npe_sup)], inc="dve")
                            bkr[2][0] = p.c("dve")
                            continue
                        p.do("vector", lambda E, h=h: E.tensor_scalar(
                            out=s_t3[:], in0=S_st[:, h, :], scalar1=b_cols[:, 6:7],
                            scalar2=None, op0=ALU.mult), inc="dve")
                        p.do("vector", lambda E, h=h: E.tensor_add(
                            S_st[:, h, :], s_t3[:], PS6[:, 2, 0:128]),
                            waits=[("pe", npe_sup)], inc="dve")
                        bkr[2][0] = p.c("dve")
                        ndve_S = p.c("dve")
                        if C_STAGE < 34:
                            continue
                        p.do("scalar", lambda E, h=h: E.activation(
                            out=Sb_st[:, h, :], in_=S_st[:, h, :], func=AF.Copy),
                            waits=[("dve", ndve_S)], inc="act")
                        if DEBUG_DUMP and h == 4 and cg == 7:
                            dbPSB = av(LN_ROWS_OFF + 3072, (128,), BF16)
                            p.do("vector", lambda E: E.tensor_copy(
                                out=dbPSB[:], in_=PSB[:, 5, :]), inc="dve")
                            dbCTX = av(LN_ROWS_OFF + 3584, (128,), BF16)
                            p.do("vector", lambda E, cxcol=cxcol: E.tensor_copy(
                                out=dbCTX[:], in_=b_ctx[:, 4, cxcol]), inc="dve")
                            dbO = av(LN_ROWS_OFF, (128,), F32)
                            dbU = av(LN_ROWS_OFF + 512, (128,), BF16)
                            dbW = av(LN_ROWS_OFF + 768, (128,), BF16)
                            dbC = av(LN_ROWS_OFF + 1024, (16,), F32)
                            dbT3 = av(LN_ROWS_OFF + 2048, (128,), F32)
                            dbON = av(LN_ROWS_OFF + 2560, (128,), BF16)
                            for (dst, srcv) in ((dbO, s_O), (dbU, s_U), (dbW, s_Wm),
                                                (dbC, b_cols), (dbT3, s_t3),
                                                (dbON, s_on)):
                                p.do("vector", lambda E, dst=dst, srcv=srcv:
                                     E.tensor_copy(out=dst[:], in_=srcv[:]),
                                     inc="dve")

                nchunks_d = p.c("dve")
                nchunks_a = p.c("act")
                if B_STAGE < 3:
                    continue
                OGS = [(2 * i, 2) for i in range(8)]
                gring_free = p.c("pe")
                for gi, (og0, ogn) in enumerate(OGS):
                    for half in range(2):
                        extra = ([("dve", nchunks_d), ("act", nchunks_a)]
                                 if gi == 0 else [])
                        p.do("gpsimd", lambda E, og0=og0, ogn=ogn, half=half:
                             E.dma_start(
                                 out=g_ring[:, half, :, 0:ogn * 128],
                                 in_=wgateT[half * 1024:(half + 1) * 1024,
                                            og0 * 128:(og0 + ogn) * 128].rearrange(
                                     "(c p) h -> p c h", p=128)),
                             waits=[("pe", gring_free)] + extra, inc="dwq", amt=16)
                    ndg = p.c("dwq")
                    for half in range(2):
                        for kc in range(8):
                            for ogl in range(ogn):
                                wts = []
                                if half == 0 and kc == 0 and ogl == 0:
                                    wts = ([("dwq", ndg), ("dve", p.c("dve")),
                                            ("act", p.c("act"))]
                                           + bw(0) + bw(1))
                                p.do("tensor", lambda E, kc=kc, ogl=ogl, half=half,
                                     scol=scol: E.matmul(
                                         PS6[:, ogl, :],
                                         g_ring[:, half, kc, ogl * 128:(ogl + 1) * 128],
                                         XLN[:, half * 8 + kc, scol],
                                         start=(half == 0 and kc == 0),
                                         stop=(half == 1 and kc == 7)),
                                     waits=wts,
                                     inc="pe" if (half == 1 and kc == 7
                                                  and ogl == ogn - 1) else None)
                    npe_g = p.c("pe")
                    gring_free = npe_g
                    for ogl in range(ogn):
                        ob = og0 + ogl
                        p.do("scalar", lambda E, ogl=ogl: E.activation(
                            out=g_tmp[:], in_=PS6[:, ogl, :], func=AF.Silu),
                            waits=[("pe", npe_g), ("dve", p.c("dve"))], inc="act")
                        bkr[ogl][1] = p.c("act")
                        nact_g = p.c("act")
                        p.do("vector", lambda E, ob=ob: E.tensor_mul(
                            gated[:, ob, :], b_ctx[:, ob, :], g_tmp[:]),
                            waits=[("act", nact_g)], inc="dve")
                ngated_d = p.c("dve")
                WGS = [(2 * i, 2) for i in range(8)]
                wring_free = p.c("pe")
                ngate_a = p.c("act")
                for gi, (dg0, dgn) in enumerate(WGS):
                    for half in range(2):
                        extra = ([("act", ngate_a)] if gi == 0 else [])
                        p.do("gpsimd", lambda E, dg0=dg0, dgn=dgn, half=half:
                             E.dma_start(
                                 out=g_ring[:, half, :, 0:dgn * 128],
                                 in_=woutT[half * 1024:(half + 1) * 1024,
                                           dg0 * 128:(dg0 + dgn) * 128].rearrange(
                                     "(c p) h -> p c h", p=128)),
                             waits=[("pe", wring_free)] + extra, inc="dwq", amt=16)
                    ndw = p.c("dwq")
                    for half in range(2):
                        for kc in range(8):
                            for dgl in range(dgn):
                                wts = []
                                if half == 0 and kc == 0 and dgl == 0:
                                    wts = ([("dwq", ndw), ("dve", ngated_d),
                                            ("act", p.c("act"))]
                                           + bw(2) + bw(3))
                                p.do("tensor", lambda E, kc=kc, dgl=dgl, half=half:
                                     E.matmul(
                                         PS6[:, 2 + dgl, :],
                                         g_ring[:, half, kc, dgl * 128:(dgl + 1) * 128],
                                         gated[:, half * 8 + kc, :],
                                         start=(half == 0 and kc == 0),
                                         stop=(half == 1 and kc == 7)),
                                     waits=wts,
                                     inc="pe" if (half == 1 and kc == 7
                                                  and dgl == dgn - 1) else None)
                    npe_w = p.c("pe")
                    wring_free = npe_w
                    for dgl in range(dgn):
                        db = dg0 + dgl
                        if SKIP_B_RESID:
                            p.do("vector", lambda E, dgl=dgl: E.tensor_copy(
                                out=s_t3[:], in_=PS6[:, 2 + dgl, 0:128]),
                                waits=[("pe", npe_w)], inc="dve")
                        else:
                            p.do("vector", lambda E, db=db, dgl=dgl, scol=scol:
                                 E.tensor_add(
                                     X[:, db, scol], X[:, db, scol],
                                     PS6[:, 2 + dgl, :]),
                                 waits=[("pe", npe_w)], inc="dve")
                        bkr[2 + dgl][0] = p.c("dve")
            return p.c("dve")

        for tile in range(N_TILES):
            prev_done = p.c("dve")
            p.do("sync", lambda E, tile=tile: E.dma_start(
                out=X[:],
                in_=x_in.rearrange("(c p) t -> p c t", p=128)[:, :, tile * T_TILE:(tile + 1) * T_TILE]),
                waits=[("dve", prev_done), ("pool", p.c("pool")), ("dout", p.c("dout"))],
                inc="dxin", amt=16)
            nxin = p.c("dxin")
            for e in ("vector", "scalar", "tensor", "gpsimd"):
                p.wait(e, [("dxin", nxin)])

            if n_sub >= 1:
                nln = ln_phase()
                attn_a_phase(nln)
            if n_sub >= 2:
                nln = ln_phase()
                ffn_phase(nln)
            if n_sub >= 3:
                nln = ln_phase()
                attn_b_phase(nln)
            if n_sub >= 4:
                nln = ln_phase()
                ffn_phase(nln)

            ndone = p.c("dve")
            npl = p.c("pool")
            p.do("sync", lambda E, tile=tile: E.dma_start(
                out=y_out.rearrange("(c p) t -> p c t", p=128)[:, :, tile * T_TILE:(tile + 1) * T_TILE],
                in_=X[:]),
                waits=[("dve", ndone), ("pool", npl)], inc="dout", amt=16)

        if DEBUG_DUMP:
            nfin_d = p.c("dve")
            nfin_p = p.c("pool")
            p.do("sync", lambda E: E.dma_start(
                out=dbg_h, in_=h_ssm[:].rearrange("p a b -> p (a b)")),
                waits=[("dve", nfin_d), ("pool", nfin_p)], inc="dout", amt=16)
            p.do("sync", lambda E: E.dma_start(out=dbg_ar, in_=arena[:]),
                 inc="dout", amt=16)
        p.wait("sync", [("dout", p.c("dout"))])

        p.finalize_and_replay(block)

    return nc


def _prep_weights(inp):
    w = {}
    w["wxT"] = np.ascontiguousarray(np.asarray(inp["L0_attn_a_W_x"]).T)
    w["wdtT"] = np.ascontiguousarray(np.asarray(inp["L0_attn_a_W_dt"]).T)
    w["bdt_c"] = np.ascontiguousarray(np.asarray(inp["L0_attn_a_b_dt"]).reshape(16, 128).T)
    w["alog"] = np.ascontiguousarray(np.asarray(inp["L0_attn_a_A_log"]))
    w["dvec_c"] = np.ascontiguousarray(np.asarray(inp["L0_attn_a_D"]).reshape(16, 128).T)
    w["wqkvT"] = np.ascontiguousarray(np.asarray(inp["L0_attn_b_w_qkv"]).T)
    w["wgateT"] = np.ascontiguousarray(np.asarray(inp["L0_attn_b_w_gate"]).T)
    w["wbetaT"] = np.ascontiguousarray(np.asarray(inp["L0_attn_b_w_beta"]).T)
    w["walphaT"] = np.ascontiguousarray(np.asarray(inp["L0_attn_b_w_alpha"]).T)
    w["logA_c"] = np.ascontiguousarray(np.asarray(inp["L0_attn_b_log_A"]).reshape(16, 1))
    w["dtb_c"] = np.ascontiguousarray(np.asarray(inp["L0_attn_b_dt_bias"]).reshape(16, 1))
    w["convw_c"] = np.ascontiguousarray(
        np.asarray(inp["L0_attn_b_conv_w"]).reshape(32, 128, 4).transpose(1, 0, 2))
    w["rmsw_r"] = np.ascontiguousarray(np.asarray(inp["L0_attn_b_rms_w"]).reshape(1, 128))
    w["woutT"] = np.ascontiguousarray(np.asarray(inp["L0_attn_b_w_out"]).T)
    w["gateT"] = np.ascontiguousarray(np.asarray(inp["L0_ffn_gate_w"]).T)
    w["w1"] = np.ascontiguousarray(np.asarray(inp["L0_ffn_w1"]))
    w["w2"] = np.ascontiguousarray(np.asarray(inp["L0_ffn_w2"]))
    w["w3"] = np.ascontiguousarray(np.asarray(inp["L0_ffn_w3"]))
    w["sw1"] = np.ascontiguousarray(np.asarray(inp["L0_ffn_sw1"]))
    w["sw2"] = np.ascontiguousarray(np.asarray(inp["L0_ffn_sw2"]))
    w["sw3"] = np.ascontiguousarray(np.asarray(inp["L0_ffn_sw3"]))
    return w


def run_cores(x_full, weights, n_sub=4, n_cores=8):
    nc = build_nc(n_sub=n_sub)
    in_maps = []
    for core in range(n_cores):
        m = dict(weights)
        m["x_fm"] = np.ascontiguousarray(np.asarray(x_full[core]).T)
        in_maps.append(m)
    res = run_bass_kernel_spmd(nc, in_maps, core_ids=list(range(n_cores)))
    out = np.stack([np.asarray(r["y_fm"]).T for r in res.results], axis=0)
    return out


def kernel(**inputs):
    w = _prep_weights(inputs)
    x = np.asarray(inputs["L0_attn_a_x"])
    out = run_cores(x, w, n_sub=4, n_cores=8)
    return out.astype(np.float32)
